# version 08
# baseline (speedup 1.0000x reference)
import numpy as np
import concourse.bass as bass
import concourse.mybir as mybir
from concourse.bass_utils import run_bass_kernel_spmd

F32 = mybir.dt.float32
I32 = mybir.dt.int32
BF16 = mybir.dt.bfloat16
ALU = mybir.AluOpType
ACT = mybir.ActivationFunctionType
AX = mybir.AxisListType

B = 4
S = 2048
D = 4096
E = 2048
TH = 1024
NCORE = 8
INV_SQRT_DK = float(1.0 / np.sqrt(128.0))
EXP_SHIFT = -12.0
EPS = 1e-5
RG = [[0, 1], [2, 3], [4, 5], [6, 7]]


class Sem:
    def __init__(self, nc, name):
        self.h = nc.alloc_semaphore(name)
        self.n = 0

    def inc(self, instr, k=1):
        instr.then_inc(self.h, k)
        self.n += k
        return self.n

    def wait(self, eng, v=None):
        eng.wait_ge(self.h, self.n if v is None else v)


def build(dbg=False):
    nc = bass.Bass("TRN2", target_bir_lowering=False, debug=False)
    DK = {"kind": "ExternalOutput"} if dbg else {}

    x_all = nc.dram_tensor("x_all", [S, D], F32, kind="ExternalInput").ap()
    x_res = nc.dram_tensor("x_res", [TH, D], F32, kind="ExternalInput").ap()
    w_q = nc.dram_tensor("w_q", [E, D], F32, kind="ExternalInput").ap()
    w_k = nc.dram_tensor("w_k", [E, D], F32, kind="ExternalInput").ap()
    w_v = nc.dram_tensor("w_v", [E, D], F32, kind="ExternalInput").ap()
    w_o = nc.dram_tensor("w_o", [D, E], F32, kind="ExternalInput").ap()
    w_f = nc.dram_tensor("w_f", [2048, D], F32, kind="ExternalInput").ap()
    bias_in = nc.dram_tensor("bias_in", [D], F32, kind="ExternalInput").ap()
    ident_in = nc.dram_tensor("ident_in", [128, 128], F32, kind="ExternalInput").ap()
    y_out = nc.dram_tensor("y_out", [TH, D], F32, kind="ExternalOutput").ap()

    s_wt = {p: nc.dram_tensor(f"s_wt_{p}", [16, 128, 8192], BF16, **DK).ap()
            for p in "qkv"}
    s_wo = nc.dram_tensor("s_wo", [32, 128, 4096], BF16, **DK).ap()
    s_wf_h = nc.dram_tensor("s_wf_half", [16, 128, 8192], BF16)
    s_wf_half = s_wf_h.ap()
    s_wf_full_h = nc.dram_tensor("s_wf", [32, 128, 8192], BF16, **DK)
    s_wf = s_wf_full_h.ap()
    s_q = nc.dram_tensor("s_q", [16, 128, S], F32, **DK).ap()
    s_k = nc.dram_tensor("s_k", [16, 128, S], F32, **DK).ap()
    s_v = nc.dram_tensor("s_v", [16, 128, S], F32, **DK).ap()
    s_rsi_h = nc.dram_tensor("s_rsi", [2, 32, 128, 1024], F32)
    s_rsi = s_rsi_h.ap()
    s_rso_h = nc.dram_tensor("s_rso", [32, 128, 1024], F32)
    s_rso = s_rso_h.ap()
    s_y1 = nc.dram_tensor("s_y1", [TH, D], F32, **DK).ap()
    s_ffn = nc.dram_tensor("s_ffn", [32, 128, 1024], F32, **DK).ap()
    if dbg:
        d_ln1 = nc.dram_tensor("d_ln1", [128, 64, 1024], BF16, **DK).ap()
        d_att = nc.dram_tensor("d_att", [128, 16, 2048], BF16, **DK).ap()
        d_rsi = nc.dram_tensor("d_rsi", [2, 32, 128, 1024], F32, **DK).ap()
        d_rso = nc.dram_tensor("d_rso", [32, 128, 1024], F32, **DK).ap()

    AR = nc.alloc_sbuf_tensor("arena", [128, 52224], F32).ap()

    def rg(off, n):
        return AR[:, off:off + n]

    def rg3(off, a, b):
        return AR[:, off:off + a * b].rearrange("p (a b) -> p a b", a=a)

    def rgb(off, n):
        return AR[:, off:off + n // 2].bitcast(BF16)

    def rgb3(off, a, b):
        return rgb(off, a * b).rearrange("p (a b) -> p a b", a=a)

    R0 = 0
    R1 = 32768
    R2 = 49152

    ident = rg(R2 + 0, 128)
    ones_col = rg(R2 + 128, 1)
    ones_row = AR[0:1, R2 + 129:R2 + 257]
    stage0 = rg(R2 + 272, 512)
    stage1 = rg(R2 + 784, 512)
    junk512 = rg(R2 + 1296, 512)
    bias_sb = rg(R2 + 1808, 32)
    sm_s = rg(R2 + 1840, 1)
    ssq8 = rg(R2 + 1841, 8)
    negm = rg(R2 + 1849, 1)
    varb = rg(R2 + 1850, 1)
    m2s = rg(R2 + 1851, 1)
    rvs = rg(R2 + 1852, 1)
    rstd = rg(R2 + 1853, 1)
    amax = rg(R2 + 1856, 32)
    scl = rg(R2 + 1888, 32)
    m0 = rg(R2 + 1920, 32)
    safe = rg(R2 + 1952, 32)
    rsafe = rg(R2 + 1984, 32)
    sums_sb = AR[0:1, R2 + 2016:R2 + 2528]
    rec_sb = AR[0:1, R2 + 2528:R2 + 3040]
    c_shift = rg(R2 + 3040, 1)
    c_scale = rg(R2 + 3041, 1)

    pb = [nc.alloc_psum_tensor(f"pb{i}", [128, 512], F32).ap() for i in range(8)]

    dI = Sem(nc, "dI")
    dW = [Sem(nc, "dW0"), Sem(nc, "dW1")]
    dX = [Sem(nc, "dX0"), Sem(nc, "dX1")]
    dSL = [Sem(nc, "dSL0"), Sem(nc, "dSL1")]
    dPL = Sem(nc, "dPL")
    dD = [Sem(nc, "dD0"), Sem(nc, "dD1")]
    dOP = [Sem(nc, "dOP0"), Sem(nc, "dOP1")]
    dXP = Sem(nc, "dXP")
    dY1 = Sem(nc, "dY1")
    gW = [Sem(nc, "gW0"), Sem(nc, "gW1")]
    gQ = [Sem(nc, "gQ0"), Sem(nc, "gQ1")]
    gD = Sem(nc, "gD")
    gY = Sem(nc, "gY")
    gDBG = Sem(nc, "gDBG")
    dOUT = Sem(nc, "dOUT")
    tS = Sem(nc, "tS")
    vS = Sem(nc, "vS")
    aS = Sem(nc, "aS")
    ccS = Sem(nc, "ccS")
    ALL_DMA_SEMS = [dI, dPL, dXP, dY1, gD, gY, gDBG, dOUT] + dW + dX + dSL + dD + dOP + gW + gQ

    sync, gps, ten, vec, sca = nc.sync, nc.gpsimd, nc.tensor, nc.vector, nc.scalar

    bank_ev = [None] * 8
    st = dict(
        xcons=[0, 0], slcons=[0, 0], qk_ev=[None, None], qk_st=[None, None],
        wocons=[0, 0], o_st=[None, None],
        dq_cons=[0, 0], dq_st=None, op_cons=[0, 0],
        wfcons=[0, 0], f_st=[None, None],
        wt_dcons=[0, 0], wt_stready=[0, 0],
        p4_y1r=0, p4_y1st=0, p4_tx=0, p6_outr=0, ssq_rd=0,
    )

    def dma(q, sem, out, in_):
        sem.inc(q.dma_start(out=out, in_=in_), 16)
        return sem.n

    def bank_wait(eng, b):
        if bank_ev[b] is not None:
            s, n = bank_ev[b]
            s.wait(eng, n)

    def barrier(engines):
        for e in engines:
            for s in [tS, vS, aS] + ALL_DMA_SEMS:
                if s.n:
                    s.wait(e)

    def ln_stats(src):
        vec.tensor_reduce(out=sm_s, in_=src, axis=AX.X, op=ALU.add)
        vec.drain()
        nm = vS.inc(vec.tensor_scalar(out=negm, in0=sm_s, scalar1=-1.0 / D,
                                      scalar2=None, op0=ALU.mult))
        if st["ssq_rd"]:
            vS.wait(sca, st["ssq_rd"])
        for ch in range(8):
            sca.activation(out=junk512, in_=src[:, ch * 512:(ch + 1) * 512],
                           func=ACT.Square, accum_out=ssq8[:, ch:ch + 1])
            sca.drain()
        vS.wait(sca, nm)
        sca.activation(out=m2s, in_=negm, func=ACT.Square)
        sca.drain()
        m2r = aS.inc(sca.activation(out=m2s, in_=m2s, func=ACT.Copy,
                                    scale=-1.0, bias=EPS))
        aS.wait(vec, m2r)
        vec.tensor_reduce(out=varb, in_=ssq8, axis=AX.X, op=ALU.add)
        vec.drain()
        st["ssq_rd"] = vS.inc(vec.tensor_scalar(out=varb, in0=varb,
                                                scalar1=1.0 / D,
                                                scalar2=m2s[:, 0:1],
                                                op0=ALU.mult, op1=ALU.add))
        vec.drain()
        rv = vS.inc(vec.reciprocal(out=rvs, in_=varb))
        vS.wait(sca, rv)
        return aS.inc(sca.activation(out=rstd, in_=rvs, func=ACT.Sqrt))

    def tx_strip(src, n_dc, writer, banks=(0, 1)):
        ngrp = n_dc // 4
        for g in range(ngrp):
            b = banks[g % 2]
            bank_wait(ten, b)
            last = None
            for j in range(4):
                dc = g * 4 + j
                last = ten.transpose(pb[b][:, j * 128:(j + 1) * 128],
                                     src[:, dc * 128:(dc + 1) * 128], ident)
            tcnt = tS.inc(last)
            use_vec = (g % 2 == 0)
            eng = vec if use_vec else sca
            tS.wait(eng, tcnt)
            ev = writer(g, pb[b], eng, use_vec)
            bank_ev[b] = ((vS if use_vec else aS),
                          (vS if use_vec else aS).inc(ev))
        return tS.n

    def tx_strip_hl(src, n_dc, hi_dst, lo_dst, banks=(0, 1), gate=None):
        ngrp = n_dc // 4
        for g in range(ngrp):
            b = banks[g % 2]
            bank_wait(ten, b)
            last = None
            for j in range(4):
                dc = g * 4 + j
                last = ten.transpose(pb[b][:, j * 128:(j + 1) * 128],
                                     src[:, dc * 128:(dc + 1) * 128], ident)
            tcnt = tS.inc(last)
            src3 = pb[b].rearrange("p (a b) -> p a b", a=4)
            tS.wait(sca, tcnt)
            if gate is not None and g == 0:
                gW[gate[0]].wait(sca, gate[1])
            hi_ap = hi_dst(g)
            ev_h = aS.inc(sca.copy(out=hi_ap, in_=src3))
            aS.wait(vec, ev_h)
            ev_l = vS.inc(vec.tensor_tensor(out=lo_dst(g), in0=src3,
                                            in1=hi_ap, op=ALU.subtract))
            bank_ev[b] = (vS, ev_l)
        return tS.n

    with nc.Block() as _blk:
        dma(sync, dI, ident, ident_in)
        dI.inc(sync.dma_start(out=bias_sb,
                              in_=bias_in.rearrange("(g p) -> p g", p=128),
                              allow_slow_non_contiguous=True), 16)
        vec.memset(ones_col, 1.0)
        vec.memset(ones_row, 1.0)
        vec.memset(c_shift, EXP_SHIFT)
        vec.memset(c_scale, INV_SQRT_DK)
        dI.wait(vec)
        vS.inc(vec.tensor_copy(out=junk512[:, 0:1], in_=ones_col))

        def weight_tx(src_dram, n_strips, n_dc, dst_dram, stg):
            wb = [rg(R1, 4096), rg(R1 + 4096, 4096)]
            half = n_dc * 128
            for r in range(n_strips):
                bi = r % 2
                if st["wt_dcons"][bi]:
                    tS.wait(sync, st["wt_dcons"][bi])
                rdy = dma(sync, dW[bi], wb[bi][:, 0:n_dc * 128],
                          src_dram[r * 128:(r + 1) * 128, :])
                dW[bi].wait(ten, rdy)
                gate = st["wt_stready"][bi] or None
                s = stg[bi]

                def hi_dst(g, _s=s):
                    return _s[:, g * 512:(g + 1) * 512].rearrange(
                        "p (a b) -> p a b", a=4)

                def lo_dst(g, _s=s, _half=half):
                    return _s[:, _half + g * 512:_half + (g + 1) * 512].rearrange(
                        "p (a b) -> p a b", a=4)

                st["wt_dcons"][bi] = tx_strip_hl(wb[bi], n_dc, hi_dst, lo_dst,
                                                 gate=gate)
                vS.wait(gps)
                aS.wait(gps)
                gps.dma_start(out=dst_dram[r][:, 0:2 * half],
                              in_=stg[bi][:, 0:2 * half]).then_inc(gW[bi].h, 16)
                gW[bi].n += 16
                st["wt_stready"][bi] = (bi, gW[bi].n)

        stgA = [rgb(R0, 8192), rgb(R0 + 4096, 8192)]
        for p, wsrc in (("q", w_q), ("k", w_k), ("v", w_v)):
            weight_tx(wsrc, 16, 32, s_wt[p], stgA)

        ln1h = rgb3(R0, 32, 1024)
        ln1l = rgb3(R0 + 16384, 32, 1024)
        xb = [rg(R1, 4096), rg(R1 + 4096, 4096)]
        wsl = [rgb(R1 + 8192, 8192), rgb(R1 + 12288, 8192)]
        qkstage = [stage0, stage1]

        for th in range(2):
            barrier([vec, sca, ten, sync])
            for stp in range(8):
                bi = stp % 2
                if st["xcons"][bi]:
                    tS.wait(sync, st["xcons"][bi])
                rdy = dma(sync, dX[bi], xb[bi],
                          x_all[th * 1024 + stp * 128:
                                th * 1024 + (stp + 1) * 128, :])
                dX[bi].wait(vec, rdy)
                dX[bi].wait(sca, rdy)
                rr = ln_stats(xb[bi])
                aS.wait(vec, rr)
                nrm = vS.inc(vec.tensor_scalar(out=xb[bi], in0=xb[bi],
                                               scalar1=negm[:, 0:1],
                                               scalar2=rstd[:, 0:1],
                                               op0=ALU.add, op1=ALU.mult))
                vS.wait(ten, nrm)

                def hi_ln(g, _stp=stp):
                    return ln1h[:, g * 4:(g + 1) * 4,
                                _stp * 128:(_stp + 1) * 128]

                def lo_ln(g, _stp=stp):
                    return ln1l[:, g * 4:(g + 1) * 4,
                                _stp * 128:(_stp + 1) * 128]

                st["xcons"][bi] = tx_strip_hl(xb[bi], 32, hi_ln, lo_ln)
            ln_v, ln_a = vS.n, aS.n
            if dbg and th == 0:
                vS.wait(gps, ln_v)
                aS.wait(gps, ln_a)
                gps.dma_start(out=d_ln1[:, 0:32, :].rearrange("p a b -> p (a b)"),
                              in_=ln1h.rearrange("p a b -> p (a b)")
                              ).then_inc(gDBG.h, 16)
                gDBG.n += 16
                gps.dma_start(out=d_ln1[:, 32:64, :].rearrange("p a b -> p (a b)"),
                              in_=ln1l.rearrange("p a b -> p (a b)")
                              ).then_inc(gDBG.h, 16)
                gDBG.n += 16

            for ip, p in enumerate("qkv"):
                dst_plane = (s_q, s_k, s_v)[ip]
                for ec in range(16):
                    gi = ip * 16 + ec
                    bi = gi % 2
                    if st["slcons"][bi]:
                        tS.wait(sync, st["slcons"][bi])
                    slrdy = dma(sync, dSL[bi], wsl[bi], s_wt[p][ec, :, :])
                    for tb in range(2):
                        k = (gi * 2 + tb) % 2
                        b = 4 + k
                        bank_wait(ten, b)
                        dSL[bi].wait(ten, slrdy)
                        if gi == 0 and tb == 0:
                            vS.wait(ten, ln_v)
                            aS.wait(ten, ln_a)
                        last = None
                        for dc in range(32):
                            hiW = wsl[bi][:, dc * 128:(dc + 1) * 128]
                            loW = wsl[bi][:, 4096 + dc * 128:
                                          4096 + (dc + 1) * 128]
                            rh = ln1h[:, dc, tb * 512:(tb + 1) * 512]
                            rl = ln1l[:, dc, tb * 512:(tb + 1) * 512]
                            ten.matmul(pb[b], hiW, rh, start=(dc == 0),
                                       stop=False)
                            ten.matmul(pb[b], hiW, rl, start=False, stop=False)
                            last = ten.matmul(pb[b], loW, rh, start=False,
                                              stop=(dc == 31))
                        mm = tS.inc(last)
                        if tb == 1:
                            st["slcons"][bi] = mm
                        tS.wait(sca, mm)
                        if st["qk_st"][k] is not None:
                            gQ[k].wait(sca, st["qk_st"][k])
                        ev = aS.inc(sca.copy(out=qkstage[k], in_=pb[b]))
                        bank_ev[b] = (aS, ev)
                        aS.wait(gps, ev)
                        gps.dma_start(
                            out=dst_plane[ec, :, th * 1024 + tb * 512:
                                          th * 1024 + (tb + 1) * 512],
                            in_=qkstage[k]).then_inc(gQ[k].h, 16)
                        gQ[k].n += 16
                        st["qk_st"][k] = gQ[k].n

        barrier([sync, ten, vec, sca, gps])
        stgB = [rgb(R0, 4096), rgb(R0 + 2048, 4096)]
        weight_tx(w_o, 32, 16, s_wo, stgB)

        barrier([sync, ten, vec, sca])
        atth = rgb3(R0, 16, 2048)
        attl = rgb3(R0 + 16384, 16, 2048)
        expb = rg3(R1, 16, 512)
        kbuf = rg(R1 + 8192, 2048)
        qbuf = rg(R1 + 10240, 2048)
        vbuf = rg(R1 + 12288, 2048)
        vtm = rg3(R1 + 14336, 16, 128)
        SCB = (0, 1, 4, 5)

        head_cons = 0
        prev_av = 0
        prev_tree = 0
        for hh in range(16):
            if hh > 0:
                tS.wait(sync, head_cons)
            dma(sync, dPL, kbuf, s_k[hh, :, :])
            dma(sync, dPL, qbuf, s_q[hh, :, :])
            plrdy = dma(sync, dPL, vbuf, s_v[hh, :, :])
            dPL.wait(ten, plrdy)
            if hh > 0:
                tS.wait(vec, head_cons)

            def wr_v(g, bank, eng, is_vec):
                src3 = bank.rearrange("p (a b) -> p a b", a=4)
                return (eng.tensor_copy(out=vtm[:, g * 4:(g + 1) * 4, :],
                                        in_=src3) if is_vec
                        else eng.copy(out=vtm[:, g * 4:(g + 1) * 4, :],
                                      in_=src3))

            tx_strip(vbuf, 16, wr_v)
            vtm_v, vtm_a = vS.n, aS.n

            for qt in range(4):
                exp_last = 0
                for kc in range(16):
                    b = SCB[kc % 4]
                    bank_wait(ten, b)
                    mm = tS.inc(ten.matmul(pb[b],
                                           kbuf[:, kc * 128:(kc + 1) * 128],
                                           qbuf[:, qt * 512:(qt + 1) * 512],
                                           start=True, stop=True))
                    tS.wait(sca, mm)
                    if kc == 0:
                        tS.wait(sca, prev_av)
                        vS.wait(sca, prev_tree)
                    ev = aS.inc(sca.activation(out=expb[:, kc, :], in_=pb[b],
                                               func=ACT.Exp,
                                               scale=c_scale[:, 0:1],
                                               bias=c_shift[:, 0:1]))
                    bank_ev[b] = (aS, ev)
                    exp_last = ev
                bav = 2 + qt % 2
                bank_wait(ten, bav)
                aS.wait(ten, exp_last)
                if qt == 0:
                    vS.wait(ten, vtm_v)
                    aS.wait(ten, vtm_a)
                last = None
                for kc in range(16):
                    last = ten.matmul(pb[bav], vtm[:, kc, :], expb[:, kc, :],
                                      start=(kc == 0), stop=(kc == 15))
                prev_av = tS.inc(last)
                aS.wait(vec, exp_last)
                tS.wait(vec, prev_av)
                vec.tensor_tensor(out=expb[:, 0:8, :], in0=expb[:, 0:8, :],
                                  in1=expb[:, 8:16, :], op=ALU.add)
                vec.drain()
                vec.tensor_tensor(out=expb[:, 0:4, :], in0=expb[:, 0:4, :],
                                  in1=expb[:, 4:8, :], op=ALU.add)
                vec.drain()
                vec.tensor_tensor(out=expb[:, 0:2, :], in0=expb[:, 0:2, :],
                                  in1=expb[:, 2:4, :], op=ALU.add)
                vec.drain()
                prev_tree = vS.inc(
                    vec.tensor_tensor(out=stage0, in0=expb[:, 0, :],
                                      in1=expb[:, 1, :], op=ALU.add))
                bank_wait(ten, 6)
                vS.wait(ten, prev_tree)
                mm_sum = tS.inc(ten.matmul(pb[6][0:1, :], ones_col, stage0,
                                           start=True, stop=True))
                tS.wait(vec, mm_sum)
                vec.tensor_copy(out=sums_sb, in_=pb[6][0:1, :])
                vec.drain()
                sums_ev = vS.inc(vec.reciprocal(out=rec_sb, in_=sums_sb))
                bank_ev[6] = (vS, sums_ev)
                bank_wait(ten, 7)
                vS.wait(ten, sums_ev)
                mm_bc = tS.inc(ten.matmul(pb[7], ones_row, rec_sb,
                                          start=True, stop=True))
                tS.wait(vec, mm_bc)
                vec.tensor_copy(out=stage1, in_=pb[7])
                vec.drain()
                ev = vS.inc(vec.tensor_tensor(out=stage1, in0=pb[bav],
                                              in1=stage1, op=ALU.mult))
                bank_ev[bav] = (vS, ev)
                bank_ev[7] = (vS, ev)
                vec.drain()
                vec.tensor_copy(out=atth[:, hh, qt * 512:(qt + 1) * 512],
                                in_=stage1)
                vec.drain()
                vS.inc(vec.tensor_tensor(
                    out=attl[:, hh, qt * 512:(qt + 1) * 512],
                    in0=stage1, in1=atth[:, hh, qt * 512:(qt + 1) * 512],
                    op=ALU.subtract))
            head_cons = tS.n
        att_done_v = vS.n
        if dbg:
            vS.wait(gps, att_done_v)
            gps.dma_start(out=d_att.rearrange("p a b -> p (a b)"),
                          in_=atth.rearrange("p a b -> p (a b)")
                          ).then_inc(gDBG.h, 16)
            gDBG.n += 16

        barrier([sync, ten, sca])
        wosl = [rgb(R1, 4096), rgb(R1 + 2048, 4096)]
        for oc in range(32):
            bi = oc % 2
            if st["wocons"][bi]:
                tS.wait(sync, st["wocons"][bi])
            slrdy = dma(sync, dSL[bi], wosl[bi], s_wo[oc, :, :])
            for tb in range(4):
                k = (oc * 4 + tb) % 2
                b = 4 + k
                bank_wait(ten, b)
                dSL[bi].wait(ten, slrdy)
                if oc == 0 and tb == 0:
                    vS.wait(ten, att_done_v)
                last = None
                for ec in range(16):
                    hiW = wosl[bi][:, ec * 128:(ec + 1) * 128]
                    loW = wosl[bi][:, 2048 + ec * 128:2048 + (ec + 1) * 128]
                    rh = atth[:, ec, tb * 512:(tb + 1) * 512]
                    rl = attl[:, ec, tb * 512:(tb + 1) * 512]
                    ten.matmul(pb[b], hiW, rh, start=(ec == 0), stop=False)
                    ten.matmul(pb[b], hiW, rl, start=False, stop=False)
                    last = ten.matmul(pb[b], loW, rh, start=False,
                                      stop=(ec == 15))
                mm = tS.inc(last)
                if tb == 3:
                    st["wocons"][bi] = mm
                tS.wait(sca, mm)
                if st["o_st"][k] is not None:
                    gQ[k].wait(sca, st["o_st"][k])
                ev = aS.inc(sca.copy(out=qkstage[k], in_=pb[b]))
                bank_ev[b] = (aS, ev)
                aS.wait(gps, ev)
                gps.dma_start(
                    out=s_rsi[tb // 2, oc, :,
                              (tb % 2) * 512:(tb % 2 + 1) * 512],
                    in_=qkstage[k]).then_inc(gQ[k].h, 16)
                gQ[k].n += 16
                st["o_st"][k] = gQ[k].n

        gps.wait_ge(gQ[0].h, gQ[0].n)
        gps.wait_ge(gQ[1].h, gQ[1].n)
        if dbg:
            gps.dma_start(out=d_rsi.rearrange("a b p t -> (a b) p t"),
                          in_=s_rsi.rearrange("a b p t -> (a b) p t")
                          ).then_inc(gDBG.h, 16)
            gDBG.n += 16
        gps.collective_compute(
            "ReduceScatter", ALU.add, replica_groups=RG,
            ins=[s_rsi_h.ap().opt()],
            outs=[s_rso_h.ap().opt()],
        ).then_inc(ccS.h, 1)
        ccS.n += 1
        if dbg:
            gps.wait_ge(ccS.h, 1)
            gps.dma_start(out=d_rso, in_=s_rso).then_inc(gDBG.h, 16)
            gDBG.n += 16

        barrier([sync, ten, vec, sca])
        wbD = [rg(R0, 4096), rg(R0 + 4096, 4096)]
        tqD = rg(R0 + 8192, 4096)
        taD = rg(R0 + 12288, 4096)
        tdD = rg(R0 + 16384, 4096)
        c1D = rg(R0 + 20480, 4096)
        c2D = rg(R0 + 24576, 4096)
        tiD = c1D.bitcast(I32)
        stD = rgb(R0 + 28672, 8192)

        def g32(apv):
            return apv.rearrange("p (g j) -> p g j", g=32)

        for r in range(16):
            bi = r % 2
            if st["dq_cons"][bi]:
                vS.wait(sync, st["dq_cons"][bi])
            rdy = dma(sync, dD[bi], wbD[bi], w_f[r * 128:(r + 1) * 128, :])
            dD[bi].wait(vec, rdy)
            vec.tensor_reduce(out=amax, in_=g32(wbD[bi]), axis=AX.X,
                              op=ALU.max, apply_absolute_value=True)
            vec.drain()
            vec.tensor_scalar(out=scl, in0=amax, scalar1=1.0 / 7.0,
                              scalar2=None, op0=ALU.mult)
            vec.drain()
            vec.tensor_scalar(out=m0, in0=scl, scalar1=0.0, scalar2=None,
                              op0=ALU.is_le)
            vec.drain()
            vec.tensor_tensor(out=safe, in0=scl, in1=m0, op=ALU.add)
            vec.drain()
            vec.reciprocal(out=rsafe, in_=safe)
            vec.drain()
            st["dq_cons"][bi] = vS.inc(vec.tensor_tensor(
                out=g32(tqD), in0=g32(wbD[bi]),
                in1=rsafe.unsqueeze(2).broadcast_to([128, 32, 128]),
                op=ALU.mult))
            vec.drain()
            vec.tensor_copy(out=tiD, in_=tqD)
            vec.drain()
            vec.tensor_copy(out=taD, in_=tiD)
            vec.drain()
            vec.tensor_tensor(out=tdD, in0=tqD, in1=taD, op=ALU.subtract)
            vec.drain()
            vec.tensor_scalar(out=c2D, in0=tdD, scalar1=-0.5, scalar2=None,
                              op0=ALU.is_lt)
            vec.drain()
            vec.tensor_tensor(out=taD, in0=taD, in1=c2D, op=ALU.subtract)
            vec.drain()
            vec.tensor_scalar(out=c1D, in0=tdD, scalar1=0.5, scalar2=None,
                              op0=ALU.is_gt)
            vec.drain()
            vec.tensor_tensor(out=taD, in0=taD, in1=c1D, op=ALU.add)
            vec.drain()
            deq_rdy = vS.inc(vec.tensor_tensor(
                out=g32(tqD), in0=g32(taD),
                in1=scl.unsqueeze(2).broadcast_to([128, 32, 128]),
                op=ALU.mult))
            vS.wait(ten, deq_rdy)
            for g in range(8):
                b = g % 2
                bank_wait(ten, b)
                last = None
                for j in range(4):
                    dc = g * 4 + j
                    last = ten.transpose(pb[b][:, j * 128:(j + 1) * 128],
                                         tqD[:, dc * 128:(dc + 1) * 128],
                                         ident)
                tcnt = tS.inc(last)
                src3 = pb[b].rearrange("p (a b) -> p a b", a=4)
                tS.wait(sca, tcnt)
                if g == 0 and st["dq_st"] is not None:
                    gD.wait(sca, st["dq_st"])
                hi_ap = stD[:, g * 512:(g + 1) * 512].rearrange(
                    "p (a b) -> p a b", a=4)
                ev_h = aS.inc(sca.copy(out=hi_ap, in_=src3))
                aS.wait(vec, ev_h)
                lo_ap = stD[:, 4096 + g * 512:4096 + (g + 1) * 512].rearrange(
                    "p (a b) -> p a b", a=4)
                ev_l = vS.inc(vec.tensor_tensor(out=lo_ap, in0=src3,
                                                in1=hi_ap, op=ALU.subtract))
                bank_ev[b] = (vS, ev_l)
            dqtx = tS.n
            tS.wait(vec, dqtx)
            vS.wait(gps)
            aS.wait(gps)
            gps.dma_start(out=s_wf_half[r, :, :], in_=stD).then_inc(gD.h, 16)
            gD.n += 16
            st["dq_st"] = gD.n
        gps.wait_ge(gD.h, gD.n)
        gps.collective_compute(
            "AllGather", ALU.bypass, replica_groups=RG,
            ins=[s_wf_h.ap().opt()],
            outs=[s_wf_full_h.ap().opt()],
        ).then_inc(ccS.h, 1)
        ccS.n += 1

        barrier([sync, ten, vec, sca, gps])
        sync.wait_ge(ccS.h, 1)
        otm = rg3(R0, 2, 4096)
        ln3h = rgb3(8192, 32, 1024)
        ln3l = rgb3(24576, 32, 1024)
        xbP = rg(40960, 4096)
        y1P = rg(45056, 4096)
        opl = [stage0, stage1]

        for blk in range(4):
            vS.wait(sca)
            aS.wait(vec)
            for oc in range(32):
                bi = oc % 2
                tS.wait(sync, st["op_cons"][bi])
                rdy = dma(sync, dOP[bi], opl[bi][:, 0:256],
                          s_rso[oc, :, blk * 256:(blk + 1) * 256])
                dOP[bi].wait(ten, rdy)
                b = oc % 2
                bank_wait(ten, b)
                last = None
                for ts in range(2):
                    last = ten.transpose(pb[b][:, ts * 128:(ts + 1) * 128],
                                         opl[bi][:, ts * 128:(ts + 1) * 128],
                                         ident)
                tcnt = tS.inc(last)
                st["op_cons"][bi] = tcnt
                use_vec = (oc % 2 == 0)
                eng = vec if use_vec else sca
                tS.wait(eng, tcnt)
                src3 = pb[b][:, 0:256].rearrange("p (a b) -> p a b", a=2)
                dst = otm[:, :, oc * 128:(oc + 1) * 128]
                ev = (eng.tensor_copy(out=dst, in_=src3) if use_vec
                      else eng.copy(out=dst, in_=src3))
                bank_ev[b] = ((vS if use_vec else aS),
                              (vS if use_vec else aS).inc(ev))
            otm_v, otm_a = vS.n, aS.n

            for ts in range(2):
                t0 = blk * 256 + ts * 128
                if st["p4_y1r"]:
                    vS.wait(sync, st["p4_y1r"])
                xrdy = dma(sync, dXP, xbP, x_res[t0:t0 + 128, :])
                ov = otm[:, ts, :]
                if ts == 0:
                    vS.wait(vec, otm_v)
                    aS.wait(vec, otm_a)
                    vS.wait(sca, otm_v)
                    aS.wait(sca, otm_a)
                rr = ln_stats(ov)
                aS.wait(vec, rr)
                vec.tensor_scalar(out=ov, in0=ov, scalar1=negm[:, 0:1],
                                  scalar2=rstd[:, 0:1],
                                  op0=ALU.add, op1=ALU.mult)
                vec.drain()
                dXP.wait(vec, xrdy)
                if st["p4_y1st"]:
                    vec.wait_ge(gY.h, st["p4_y1st"])
                y1r = vS.inc(vec.tensor_tensor(out=y1P, in0=ov, in1=xbP,
                                               op=ALU.add))
                st["p4_y1r"] = y1r
                vS.wait(gps, y1r)
                gps.dma_start(out=s_y1[t0:t0 + 128, :], in_=y1P
                              ).then_inc(gY.h, 16)
                gY.n += 16
                st["p4_y1st"] = gY.n
                vS.wait(sca, y1r)
                vec.drain()
                rr3 = ln_stats(y1P)
                aS.wait(vec, rr3)
                if st["p4_tx"]:
                    tS.wait(vec, st["p4_tx"])
                ln3r = vS.inc(vec.tensor_scalar(out=ov, in0=y1P,
                                                scalar1=negm[:, 0:1],
                                                scalar2=rstd[:, 0:1],
                                                op0=ALU.add, op1=ALU.mult))
                vS.wait(ten, ln3r)

                def hi3(g, _t0=t0):
                    return ln3h[:, g * 4:(g + 1) * 4, _t0:_t0 + 128]

                def lo3(g, _t0=t0):
                    return ln3l[:, g * 4:(g + 1) * 4, _t0:_t0 + 128]

                st["p4_tx"] = tx_strip_hl(ov, 32, hi3, lo3, banks=(2, 3))

        barrier([sync, ten, vec, sca, gps])
        sync.wait_ge(ccS.h, 2)
        wfl = [rgb(40960, 8192), rgb(45056, 8192)]
        for oc in range(32):
            bi = oc % 2
            if st["wfcons"][bi]:
                tS.wait(sync, st["wfcons"][bi])
            slrdy = dma(sync, dSL[bi], wfl[bi], s_wf[oc, :, :])
            for tb in range(2):
                k = (oc * 2 + tb) % 2
                b = 4 + k
                bank_wait(ten, b)
                dSL[bi].wait(ten, slrdy)
                last = None
                for dc in range(32):
                    hiW = wfl[bi][:, dc * 128:(dc + 1) * 128]
                    loW = wfl[bi][:, 4096 + dc * 128:4096 + (dc + 1) * 128]
                    rh = ln3h[:, dc, tb * 512:(tb + 1) * 512]
                    rl = ln3l[:, dc, tb * 512:(tb + 1) * 512]
                    ten.matmul(pb[b], hiW, rh, start=(dc == 0), stop=False)
                    ten.matmul(pb[b], hiW, rl, start=False, stop=False)
                    last = ten.matmul(pb[b], loW, rh, start=False,
                                      stop=(dc == 31))
                mm = tS.inc(last)
                if tb == 1:
                    st["wfcons"][bi] = mm
                tS.wait(sca, mm)
                if st["f_st"][k] is not None:
                    gQ[k].wait(sca, st["f_st"][k])
                ev = aS.inc(sca.activation(out=qkstage[k], in_=pb[b],
                                           func=ACT.Identity, scale=1.0,
                                           bias=bias_sb[:, oc:oc + 1]))
                bank_ev[b] = (aS, ev)
                aS.wait(gps, ev)
                gps.dma_start(out=s_ffn[oc, :, tb * 512:(tb + 1) * 512],
                              in_=qkstage[k]).then_inc(gQ[k].h, 16)
                gQ[k].n += 16
                st["f_st"][k] = gQ[k].n

        barrier([sync, ten, vec, sca, gps])
        ftm = rg3(R0, 4, 4096)
        y1b = rg(R1, 4096)
        outb = rg(R1 + 4096, 4096)
        for blk in range(2):
            vS.wait(sca)
            aS.wait(vec)
            for oc in range(32):
                bi = oc % 2
                tS.wait(sync, st["op_cons"][bi])
                rdy = dma(sync, dOP[bi], opl[bi],
                          s_ffn[oc, :, blk * 512:(blk + 1) * 512])
                dOP[bi].wait(ten, rdy)
                b = oc % 2
                bank_wait(ten, b)
                last = None
                for ts in range(4):
                    last = ten.transpose(pb[b][:, ts * 128:(ts + 1) * 128],
                                         opl[bi][:, ts * 128:(ts + 1) * 128],
                                         ident)
                tcnt = tS.inc(last)
                st["op_cons"][bi] = tcnt
                tS.wait(sca, tcnt)
                src3 = pb[b].rearrange("p (a b) -> p a b", a=4)
                dst = ftm[:, :, oc * 128:(oc + 1) * 128]
                ev = sca.copy(out=dst, in_=src3)
                bank_ev[b] = (aS, aS.inc(ev))
            ftm_v, ftm_a = vS.n, aS.n
            for ts in range(4):
                t0 = blk * 512 + ts * 128
                if st["p6_outr"]:
                    vS.wait(sync, st["p6_outr"])
                yrdy = dma(sync, dY1, y1b, s_y1[t0:t0 + 128, :])
                fv = ftm[:, ts, :]
                if ts == 0:
                    vS.wait(vec, ftm_v)
                    aS.wait(vec, ftm_a)
                    vS.wait(sca, ftm_v)
                    aS.wait(sca, ftm_a)
                rr = ln_stats(fv)
                aS.wait(vec, rr)
                vec.tensor_scalar(out=fv, in0=fv, scalar1=negm[:, 0:1],
                                  scalar2=rstd[:, 0:1],
                                  op0=ALU.add, op1=ALU.mult)
                vec.drain()
                dY1.wait(vec, yrdy)
                if blk + ts > 0:
                    vec.wait_ge(dOUT.h, dOUT.n)
                outr = vS.inc(vec.tensor_tensor(out=outb, in0=fv, in1=y1b,
                                                op=ALU.add))
                st["p6_outr"] = outr
                vS.wait(gps, outr)
                gps.dma_start(out=y_out[t0:t0 + 128, :], in_=outb
                              ).then_inc(dOUT.h, 16)
                dOUT.n += 16

        gps.wait_ge(dOUT.h, dOUT.n)
        sync.wait_ge(dOUT.h, dOUT.n)

    return nc


def kernel(L0_attn_x, L0_attn_w_q, L0_attn_w_k, L0_attn_w_v, L0_attn_w_o,
           L0_ffn_weight, L0_ffn_bias):
    x = np.ascontiguousarray(np.asarray(L0_attn_x, dtype=np.float32))
    wq = np.ascontiguousarray(np.asarray(L0_attn_w_q, dtype=np.float32))
    wk = np.ascontiguousarray(np.asarray(L0_attn_w_k, dtype=np.float32))
    wv = np.ascontiguousarray(np.asarray(L0_attn_w_v, dtype=np.float32))
    wo = np.ascontiguousarray(np.asarray(L0_attn_w_o, dtype=np.float32))
    wf = np.ascontiguousarray(np.asarray(L0_ffn_weight, dtype=np.float32))
    bb = np.ascontiguousarray(np.asarray(L0_ffn_bias, dtype=np.float32))
    ident = np.eye(128, dtype=np.float32)

    nc = build()
    in_maps = []
    for c in range(NCORE):
        b, h = c // 2, c % 2
        sl = slice(h * E, (h + 1) * E)
        in_maps.append({
            "x_all": x[b],
            "x_res": np.ascontiguousarray(x[b, h * TH:(h + 1) * TH]),
            "w_q": np.ascontiguousarray(wq[sl]),
            "w_k": np.ascontiguousarray(wk[sl]),
            "w_v": np.ascontiguousarray(wv[sl]),
            "w_o": np.ascontiguousarray(wo[:, sl]),
            "w_f": np.ascontiguousarray(wf[h * 2048:(h + 1) * 2048]),
            "bias_in": bb,
            "ident_in": ident,
        })
    res = run_bass_kernel_spmd(nc, in_maps, core_ids=list(range(NCORE)))
    out = np.empty((B, S, D), dtype=np.float32)
    for c in range(NCORE):
        b, h = c // 2, c % 2
        out[b, h * TH:(h + 1) * TH] = np.asarray(res.results[c]["y_out"])
    return out


# version 09
# speedup vs baseline: 1.0351x; 1.0351x over previous
import numpy as np
import concourse.bass as bass
import concourse.mybir as mybir
from concourse.bass_utils import run_bass_kernel_spmd

F32 = mybir.dt.float32
I32 = mybir.dt.int32
BF16 = mybir.dt.bfloat16
ALU = mybir.AluOpType
ACT = mybir.ActivationFunctionType
AX = mybir.AxisListType

B = 4
S = 2048
D = 4096
E = 2048
TH = 1024
NCORE = 8
INV_SQRT_DK = float(1.0 / np.sqrt(128.0))
EXP_SHIFT = -12.0
EPS = 1e-5
RG = [[0, 1], [2, 3], [4, 5], [6, 7]]


class Sem:
    def __init__(self, nc, name):
        self.h = nc.alloc_semaphore(name)
        self.n = 0

    def inc(self, instr, k=1):
        instr.then_inc(self.h, k)
        self.n += k
        return self.n

    def wait(self, eng, v=None):
        eng.wait_ge(self.h, self.n if v is None else v)


def build(dbg=False):
    nc = bass.Bass("TRN2", target_bir_lowering=False, debug=False)
    DK = {"kind": "ExternalOutput"} if dbg else {}

    x_all = nc.dram_tensor("x_all", [S, D], F32, kind="ExternalInput").ap()
    x_res = nc.dram_tensor("x_res", [TH, D], F32, kind="ExternalInput").ap()
    w_q = nc.dram_tensor("w_q", [E, D], F32, kind="ExternalInput").ap()
    w_k = nc.dram_tensor("w_k", [E, D], F32, kind="ExternalInput").ap()
    w_v = nc.dram_tensor("w_v", [E, D], F32, kind="ExternalInput").ap()
    w_o = nc.dram_tensor("w_o", [D, E], F32, kind="ExternalInput").ap()
    w_f = nc.dram_tensor("w_f", [D, D], F32, kind="ExternalInput").ap()
    bias_in = nc.dram_tensor("bias_in", [D], F32, kind="ExternalInput").ap()
    ident_in = nc.dram_tensor("ident_in", [128, 128], F32, kind="ExternalInput").ap()
    y_out = nc.dram_tensor("y_out", [TH, D], F32, kind="ExternalOutput").ap()

    s_wt = {p: nc.dram_tensor(f"s_wt_{p}", [16, 128, 8192], BF16, **DK).ap()
            for p in "qkv"}
    s_wo = nc.dram_tensor("s_wo", [32, 128, 4096], BF16, **DK).ap()
    s_wf = nc.dram_tensor("s_wf", [32, 128, 8192], BF16, **DK).ap()
    s_q = nc.dram_tensor("s_q", [16, 128, S], F32, **DK).ap()
    s_k = nc.dram_tensor("s_k", [16, 128, S], F32, **DK).ap()
    s_v = nc.dram_tensor("s_v", [16, 128, S], F32, **DK).ap()
    s_rsi_h = nc.dram_tensor("s_rsi", [2, 32, 128, 1024], F32)
    s_rsi = s_rsi_h.ap()
    s_rso_h = nc.dram_tensor("s_rso", [32, 128, 1024], F32)
    s_rso = s_rso_h.ap()
    s_y1 = nc.dram_tensor("s_y1", [TH, D], F32, **DK).ap()
    s_ffn = nc.dram_tensor("s_ffn", [32, 128, 1024], F32, **DK).ap()
    if dbg:
        d_ln1 = nc.dram_tensor("d_ln1", [128, 64, 1024], BF16, **DK).ap()
        d_att = nc.dram_tensor("d_att", [128, 16, 2048], BF16, **DK).ap()
        d_rsi = nc.dram_tensor("d_rsi", [2, 32, 128, 1024], F32, **DK).ap()
        d_rso = nc.dram_tensor("d_rso", [32, 128, 1024], F32, **DK).ap()

    AR = nc.alloc_sbuf_tensor("arena", [128, 52224], F32).ap()

    def rg(off, n):
        return AR[:, off:off + n]

    def rg3(off, a, b):
        return AR[:, off:off + a * b].rearrange("p (a b) -> p a b", a=a)

    def rgb(off, n):
        return AR[:, off:off + n // 2].bitcast(BF16)

    def rgb3(off, a, b):
        return rgb(off, a * b).rearrange("p (a b) -> p a b", a=a)

    R0 = 0
    R1 = 32768
    R2 = 49152

    ident = rg(R2 + 0, 128)
    ones_col = rg(R2 + 128, 1)
    ones_row = AR[0:1, R2 + 129:R2 + 257]
    stage0 = rg(R2 + 272, 512)
    stage1 = rg(R2 + 784, 512)
    junk512 = rg(R2 + 1296, 512)
    bias_sb = rg(R2 + 1808, 32)
    sm_s = rg(R2 + 1840, 1)
    ssq8 = rg(R2 + 1841, 8)
    negm = rg(R2 + 1849, 1)
    varb = rg(R2 + 1850, 1)
    m2s = rg(R2 + 1851, 1)
    rvs = rg(R2 + 1852, 1)
    rstd = rg(R2 + 1853, 1)
    amax = rg(R2 + 1856, 32)
    scl = rg(R2 + 1888, 32)
    m0 = rg(R2 + 1920, 32)
    safe = rg(R2 + 1952, 32)
    rsafe = rg(R2 + 1984, 32)
    sums_sb = AR[0:1, R2 + 2016:R2 + 2528]
    rec_sb = AR[0:1, R2 + 2528:R2 + 3040]
    c_shift = rg(R2 + 3040, 1)
    c_scale = rg(R2 + 3041, 1)

    pb = [nc.alloc_psum_tensor(f"pb{i}", [128, 512], F32).ap() for i in range(8)]

    dI = Sem(nc, "dI")
    dW = [Sem(nc, "dW0"), Sem(nc, "dW1")]
    dX = [Sem(nc, "dX0"), Sem(nc, "dX1")]
    dSL = [Sem(nc, "dSL0"), Sem(nc, "dSL1")]
    dPL = Sem(nc, "dPL")
    dD = [Sem(nc, "dD0"), Sem(nc, "dD1")]
    dOP = [Sem(nc, "dOP0"), Sem(nc, "dOP1")]
    dXP = Sem(nc, "dXP")
    dY1 = Sem(nc, "dY1")
    gW = [Sem(nc, "gW0"), Sem(nc, "gW1")]
    gQ = [Sem(nc, "gQ0"), Sem(nc, "gQ1")]
    gD = Sem(nc, "gD")
    gY = Sem(nc, "gY")
    gDBG = Sem(nc, "gDBG")
    dOUT = Sem(nc, "dOUT")
    tS = Sem(nc, "tS")
    vS = Sem(nc, "vS")
    aS = Sem(nc, "aS")
    ccS = Sem(nc, "ccS")
    ALL_DMA_SEMS = [dI, dPL, dXP, dY1, gD, gY, gDBG, dOUT] + dW + dX + dSL + dD + dOP + gW + gQ

    sync, gps, ten, vec, sca = nc.sync, nc.gpsimd, nc.tensor, nc.vector, nc.scalar

    bank_ev = [None] * 8
    st = dict(
        xcons=[0, 0], slcons=[0, 0], qk_ev=[None, None], qk_st=[None, None],
        wocons=[0, 0], o_st=[None, None],
        dq_cons=[0, 0], dq_st=None, op_cons=[0, 0],
        wfcons=[0, 0], f_st=[None, None],
        wt_dcons=[0, 0], wt_stready=[0, 0],
        p4_y1r=0, p4_y1st=0, p4_tx=0, p6_outr=0, ssq_rd=0,
    )

    def dma(q, sem, out, in_):
        sem.inc(q.dma_start(out=out, in_=in_), 16)
        return sem.n

    def bank_wait(eng, b):
        if bank_ev[b] is not None:
            s, n = bank_ev[b]
            s.wait(eng, n)

    def barrier(engines):
        for e in engines:
            for s in [tS, vS, aS] + ALL_DMA_SEMS:
                if s.n:
                    s.wait(e)

    def ln_stats(src):
        vec.tensor_reduce(out=sm_s, in_=src, axis=AX.X, op=ALU.add)
        vec.drain()
        nm = vS.inc(vec.tensor_scalar(out=negm, in0=sm_s, scalar1=-1.0 / D,
                                      scalar2=None, op0=ALU.mult))
        if st["ssq_rd"]:
            vS.wait(sca, st["ssq_rd"])
        for ch in range(8):
            sca.activation(out=junk512, in_=src[:, ch * 512:(ch + 1) * 512],
                           func=ACT.Square, accum_out=ssq8[:, ch:ch + 1])
            sca.drain()
        vS.wait(sca, nm)
        sca.activation(out=m2s, in_=negm, func=ACT.Square)
        sca.drain()
        m2r = aS.inc(sca.activation(out=m2s, in_=m2s, func=ACT.Copy,
                                    scale=-1.0, bias=EPS))
        aS.wait(vec, m2r)
        vec.tensor_reduce(out=varb, in_=ssq8, axis=AX.X, op=ALU.add)
        vec.drain()
        st["ssq_rd"] = vS.inc(vec.tensor_scalar(out=varb, in0=varb,
                                                scalar1=1.0 / D,
                                                scalar2=m2s[:, 0:1],
                                                op0=ALU.mult, op1=ALU.add))
        vec.drain()
        rv = vS.inc(vec.reciprocal(out=rvs, in_=varb))
        vS.wait(sca, rv)
        return aS.inc(sca.activation(out=rstd, in_=rvs, func=ACT.Sqrt))

    def tx_strip(src, n_dc, writer, banks=(0, 1)):
        ngrp = n_dc // 4
        for g in range(ngrp):
            b = banks[g % 2]
            bank_wait(ten, b)
            last = None
            for j in range(4):
                dc = g * 4 + j
                last = ten.transpose(pb[b][:, j * 128:(j + 1) * 128],
                                     src[:, dc * 128:(dc + 1) * 128], ident)
            tcnt = tS.inc(last)
            use_vec = (g % 2 == 0)
            eng = vec if use_vec else sca
            tS.wait(eng, tcnt)
            ev = writer(g, pb[b], eng, use_vec)
            bank_ev[b] = ((vS if use_vec else aS),
                          (vS if use_vec else aS).inc(ev))
        return tS.n

    def tx_strip_hl(src, n_dc, hi_dst, lo_dst, banks=(0, 1), gate=None):
        ngrp = n_dc // 4
        for g in range(ngrp):
            b = banks[g % 2]
            bank_wait(ten, b)
            last = None
            for j in range(4):
                dc = g * 4 + j
                last = ten.transpose(pb[b][:, j * 128:(j + 1) * 128],
                                     src[:, dc * 128:(dc + 1) * 128], ident)
            tcnt = tS.inc(last)
            src3 = pb[b].rearrange("p (a b) -> p a b", a=4)
            tS.wait(sca, tcnt)
            if gate is not None and g == 0:
                gW[gate[0]].wait(sca, gate[1])
            hi_ap = hi_dst(g)
            ev_h = aS.inc(sca.copy(out=hi_ap, in_=src3))
            aS.wait(vec, ev_h)
            ev_l = vS.inc(vec.tensor_tensor(out=lo_dst(g), in0=src3,
                                            in1=hi_ap, op=ALU.subtract))
            bank_ev[b] = (vS, ev_l)
        return tS.n

    with nc.Block() as _blk:
        dma(sync, dI, ident, ident_in)
        dI.inc(sync.dma_start(out=bias_sb,
                              in_=bias_in.rearrange("(g p) -> p g", p=128),
                              allow_slow_non_contiguous=True), 16)
        vec.memset(ones_col, 1.0)
        vec.memset(ones_row, 1.0)
        vec.memset(c_shift, EXP_SHIFT)
        vec.memset(c_scale, INV_SQRT_DK)
        dI.wait(vec)
        vS.inc(vec.tensor_copy(out=junk512[:, 0:1], in_=ones_col))

        def weight_tx(src_dram, n_strips, n_dc, dst_dram, stg):
            wb = [rg(R1, 4096), rg(R1 + 4096, 4096)]
            half = n_dc * 128
            for r in range(n_strips):
                bi = r % 2
                if st["wt_dcons"][bi]:
                    tS.wait(sync, st["wt_dcons"][bi])
                rdy = dma(sync, dW[bi], wb[bi][:, 0:n_dc * 128],
                          src_dram[r * 128:(r + 1) * 128, :])
                dW[bi].wait(ten, rdy)
                gate = st["wt_stready"][bi] or None
                s = stg[bi]

                def hi_dst(g, _s=s):
                    return _s[:, g * 512:(g + 1) * 512].rearrange(
                        "p (a b) -> p a b", a=4)

                def lo_dst(g, _s=s, _half=half):
                    return _s[:, _half + g * 512:_half + (g + 1) * 512].rearrange(
                        "p (a b) -> p a b", a=4)

                st["wt_dcons"][bi] = tx_strip_hl(wb[bi], n_dc, hi_dst, lo_dst,
                                                 gate=gate)
                vS.wait(gps)
                aS.wait(gps)
                gps.dma_start(out=dst_dram[r][:, 0:2 * half],
                              in_=stg[bi][:, 0:2 * half]).then_inc(gW[bi].h, 16)
                gW[bi].n += 16
                st["wt_stready"][bi] = (bi, gW[bi].n)

        stgA = [rgb(R0, 8192), rgb(R0 + 4096, 8192)]
        for p, wsrc in (("q", w_q), ("k", w_k), ("v", w_v)):
            weight_tx(wsrc, 16, 32, s_wt[p], stgA)

        ln1h = rgb3(R0, 32, 1024)
        ln1l = rgb3(R0 + 16384, 32, 1024)
        xb = [rg(R1, 4096), rg(R1 + 4096, 4096)]
        wsl = [rgb(R1 + 8192, 8192), rgb(R1 + 12288, 8192)]
        qkstage = [stage0, stage1]

        for th in range(2):
            barrier([vec, sca, ten, sync])
            for stp in range(8):
                bi = stp % 2
                if st["xcons"][bi]:
                    tS.wait(sync, st["xcons"][bi])
                rdy = dma(sync, dX[bi], xb[bi],
                          x_all[th * 1024 + stp * 128:
                                th * 1024 + (stp + 1) * 128, :])
                dX[bi].wait(vec, rdy)
                dX[bi].wait(sca, rdy)
                rr = ln_stats(xb[bi])
                aS.wait(vec, rr)
                nrm = vS.inc(vec.tensor_scalar(out=xb[bi], in0=xb[bi],
                                               scalar1=negm[:, 0:1],
                                               scalar2=rstd[:, 0:1],
                                               op0=ALU.add, op1=ALU.mult))
                vS.wait(ten, nrm)

                def hi_ln(g, _stp=stp):
                    return ln1h[:, g * 4:(g + 1) * 4,
                                _stp * 128:(_stp + 1) * 128]

                def lo_ln(g, _stp=stp):
                    return ln1l[:, g * 4:(g + 1) * 4,
                                _stp * 128:(_stp + 1) * 128]

                st["xcons"][bi] = tx_strip_hl(xb[bi], 32, hi_ln, lo_ln)
            ln_v, ln_a = vS.n, aS.n
            if dbg and th == 0:
                vS.wait(gps, ln_v)
                aS.wait(gps, ln_a)
                gps.dma_start(out=d_ln1[:, 0:32, :].rearrange("p a b -> p (a b)"),
                              in_=ln1h.rearrange("p a b -> p (a b)")
                              ).then_inc(gDBG.h, 16)
                gDBG.n += 16
                gps.dma_start(out=d_ln1[:, 32:64, :].rearrange("p a b -> p (a b)"),
                              in_=ln1l.rearrange("p a b -> p (a b)")
                              ).then_inc(gDBG.h, 16)
                gDBG.n += 16

            for ip, p in enumerate("qkv"):
                dst_plane = (s_q, s_k, s_v)[ip]
                for ec in range(16):
                    gi = ip * 16 + ec
                    bi = gi % 2
                    if st["slcons"][bi]:
                        tS.wait(sync, st["slcons"][bi])
                    slrdy = dma(sync, dSL[bi], wsl[bi], s_wt[p][ec, :, :])
                    for tb in range(2):
                        k = (gi * 2 + tb) % 2
                        b = 4 + k
                        bank_wait(ten, b)
                        dSL[bi].wait(ten, slrdy)
                        if gi == 0 and tb == 0:
                            vS.wait(ten, ln_v)
                            aS.wait(ten, ln_a)
                        last = None
                        for dc in range(32):
                            hiW = wsl[bi][:, dc * 128:(dc + 1) * 128]
                            loW = wsl[bi][:, 4096 + dc * 128:
                                          4096 + (dc + 1) * 128]
                            rh = ln1h[:, dc, tb * 512:(tb + 1) * 512]
                            rl = ln1l[:, dc, tb * 512:(tb + 1) * 512]
                            ten.matmul(pb[b], hiW, rh, start=(dc == 0),
                                       stop=False)
                            ten.matmul(pb[b], hiW, rl, start=False, stop=False)
                            last = ten.matmul(pb[b], loW, rh, start=False,
                                              stop=(dc == 31))
                        mm = tS.inc(last)
                        if tb == 1:
                            st["slcons"][bi] = mm
                        tS.wait(sca, mm)
                        if st["qk_st"][k] is not None:
                            gQ[k].wait(sca, st["qk_st"][k])
                        ev = aS.inc(sca.copy(out=qkstage[k], in_=pb[b]))
                        bank_ev[b] = (aS, ev)
                        aS.wait(gps, ev)
                        gps.dma_start(
                            out=dst_plane[ec, :, th * 1024 + tb * 512:
                                          th * 1024 + (tb + 1) * 512],
                            in_=qkstage[k]).then_inc(gQ[k].h, 16)
                        gQ[k].n += 16
                        st["qk_st"][k] = gQ[k].n

        barrier([sync, ten, vec, sca, gps])
        stgB = [rgb(R0, 4096), rgb(R0 + 2048, 4096)]
        weight_tx(w_o, 32, 16, s_wo, stgB)

        barrier([sync, ten, vec, sca])
        atth = rgb3(R0, 16, 2048)
        attl = rgb3(R0 + 16384, 16, 2048)
        expb = rg3(R1, 16, 512)
        kbuf = rg(R1 + 8192, 2048)
        qbuf = rg(R1 + 10240, 2048)
        vbuf = rg(R1 + 12288, 2048)
        vtm = rg3(R1 + 14336, 16, 128)
        SCB = (0, 1, 4, 5)

        head_cons = 0
        prev_av = 0
        prev_tree = 0
        for hh in range(16):
            if hh > 0:
                tS.wait(sync, head_cons)
            dma(sync, dPL, kbuf, s_k[hh, :, :])
            dma(sync, dPL, qbuf, s_q[hh, :, :])
            plrdy = dma(sync, dPL, vbuf, s_v[hh, :, :])
            dPL.wait(ten, plrdy)
            if hh > 0:
                tS.wait(vec, head_cons)

            def wr_v(g, bank, eng, is_vec):
                src3 = bank.rearrange("p (a b) -> p a b", a=4)
                return (eng.tensor_copy(out=vtm[:, g * 4:(g + 1) * 4, :],
                                        in_=src3) if is_vec
                        else eng.copy(out=vtm[:, g * 4:(g + 1) * 4, :],
                                      in_=src3))

            tx_strip(vbuf, 16, wr_v)
            vtm_v, vtm_a = vS.n, aS.n

            for qt in range(4):
                exp_last = 0
                for kc in range(16):
                    b = SCB[kc % 4]
                    bank_wait(ten, b)
                    mm = tS.inc(ten.matmul(pb[b],
                                           kbuf[:, kc * 128:(kc + 1) * 128],
                                           qbuf[:, qt * 512:(qt + 1) * 512],
                                           start=True, stop=True))
                    tS.wait(sca, mm)
                    if kc == 0:
                        tS.wait(sca, prev_av)
                        vS.wait(sca, prev_tree)
                    ev = aS.inc(sca.activation(out=expb[:, kc, :], in_=pb[b],
                                               func=ACT.Exp,
                                               scale=c_scale[:, 0:1],
                                               bias=c_shift[:, 0:1]))
                    bank_ev[b] = (aS, ev)
                    exp_last = ev
                bav = 2 + qt % 2
                bank_wait(ten, bav)
                aS.wait(ten, exp_last)
                if qt == 0:
                    vS.wait(ten, vtm_v)
                    aS.wait(ten, vtm_a)
                last = None
                for kc in range(16):
                    last = ten.matmul(pb[bav], vtm[:, kc, :], expb[:, kc, :],
                                      start=(kc == 0), stop=(kc == 15))
                prev_av = tS.inc(last)
                aS.wait(vec, exp_last)
                tS.wait(vec, prev_av)
                vec.tensor_tensor(out=expb[:, 0:8, :], in0=expb[:, 0:8, :],
                                  in1=expb[:, 8:16, :], op=ALU.add)
                vec.drain()
                vec.tensor_tensor(out=expb[:, 0:4, :], in0=expb[:, 0:4, :],
                                  in1=expb[:, 4:8, :], op=ALU.add)
                vec.drain()
                vec.tensor_tensor(out=expb[:, 0:2, :], in0=expb[:, 0:2, :],
                                  in1=expb[:, 2:4, :], op=ALU.add)
                vec.drain()
                prev_tree = vS.inc(
                    vec.tensor_tensor(out=stage0, in0=expb[:, 0, :],
                                      in1=expb[:, 1, :], op=ALU.add))
                bank_wait(ten, 6)
                vS.wait(ten, prev_tree)
                mm_sum = tS.inc(ten.matmul(pb[6][0:1, :], ones_col, stage0,
                                           start=True, stop=True))
                tS.wait(vec, mm_sum)
                vec.tensor_copy(out=sums_sb, in_=pb[6][0:1, :])
                vec.drain()
                sums_ev = vS.inc(vec.reciprocal(out=rec_sb, in_=sums_sb))
                bank_ev[6] = (vS, sums_ev)
                bank_wait(ten, 7)
                vS.wait(ten, sums_ev)
                mm_bc = tS.inc(ten.matmul(pb[7], ones_row, rec_sb,
                                          start=True, stop=True))
                tS.wait(vec, mm_bc)
                vec.tensor_copy(out=stage1, in_=pb[7])
                vec.drain()
                ev = vS.inc(vec.tensor_tensor(out=stage1, in0=pb[bav],
                                              in1=stage1, op=ALU.mult))
                bank_ev[bav] = (vS, ev)
                bank_ev[7] = (vS, ev)
                vec.drain()
                vec.tensor_copy(out=atth[:, hh, qt * 512:(qt + 1) * 512],
                                in_=stage1)
                vec.drain()
                vS.inc(vec.tensor_tensor(
                    out=attl[:, hh, qt * 512:(qt + 1) * 512],
                    in0=stage1, in1=atth[:, hh, qt * 512:(qt + 1) * 512],
                    op=ALU.subtract))
            head_cons = tS.n
        att_done_v = vS.n
        if dbg:
            vS.wait(gps, att_done_v)
            gps.dma_start(out=d_att.rearrange("p a b -> p (a b)"),
                          in_=atth.rearrange("p a b -> p (a b)")
                          ).then_inc(gDBG.h, 16)
            gDBG.n += 16

        barrier([sync, ten, sca])
        wosl = [rgb(R1, 4096), rgb(R1 + 2048, 4096)]
        for oc in range(32):
            bi = oc % 2
            if st["wocons"][bi]:
                tS.wait(sync, st["wocons"][bi])
            slrdy = dma(sync, dSL[bi], wosl[bi], s_wo[oc, :, :])
            for tb in range(4):
                k = (oc * 4 + tb) % 2
                b = 4 + k
                bank_wait(ten, b)
                dSL[bi].wait(ten, slrdy)
                if oc == 0 and tb == 0:
                    vS.wait(ten, att_done_v)
                last = None
                for ec in range(16):
                    hiW = wosl[bi][:, ec * 128:(ec + 1) * 128]
                    loW = wosl[bi][:, 2048 + ec * 128:2048 + (ec + 1) * 128]
                    rh = atth[:, ec, tb * 512:(tb + 1) * 512]
                    rl = attl[:, ec, tb * 512:(tb + 1) * 512]
                    ten.matmul(pb[b], hiW, rh, start=(ec == 0), stop=False)
                    ten.matmul(pb[b], hiW, rl, start=False, stop=False)
                    last = ten.matmul(pb[b], loW, rh, start=False,
                                      stop=(ec == 15))
                mm = tS.inc(last)
                if tb == 3:
                    st["wocons"][bi] = mm
                tS.wait(sca, mm)
                if st["o_st"][k] is not None:
                    gQ[k].wait(sca, st["o_st"][k])
                ev = aS.inc(sca.copy(out=qkstage[k], in_=pb[b]))
                bank_ev[b] = (aS, ev)
                aS.wait(gps, ev)
                gps.dma_start(
                    out=s_rsi[tb // 2, oc, :,
                              (tb % 2) * 512:(tb % 2 + 1) * 512],
                    in_=qkstage[k]).then_inc(gQ[k].h, 16)
                gQ[k].n += 16
                st["o_st"][k] = gQ[k].n

        gps.wait_ge(gQ[0].h, gQ[0].n)
        gps.wait_ge(gQ[1].h, gQ[1].n)
        if dbg:
            gps.dma_start(out=d_rsi.rearrange("a b p t -> (a b) p t"),
                          in_=s_rsi.rearrange("a b p t -> (a b) p t")
                          ).then_inc(gDBG.h, 16)
            gDBG.n += 16
        gps.collective_compute(
            "ReduceScatter", ALU.add, replica_groups=RG,
            ins=[s_rsi_h.ap().opt()],
            outs=[s_rso_h.ap().opt()],
        ).then_inc(ccS.h, 1)
        ccS.n += 1
        if dbg:
            gps.wait_ge(ccS.h, 1)
            gps.dma_start(out=d_rso, in_=s_rso).then_inc(gDBG.h, 16)
            gDBG.n += 16

        barrier([sync, ten, vec, sca])
        wbD = [rg(R0, 4096), rg(R0 + 4096, 4096)]
        tqD = rg(R0 + 8192, 4096)
        taD = rg(R0 + 12288, 4096)
        tdD = rg(R0 + 16384, 4096)
        c1D = rg(R0 + 20480, 4096)
        c2D = rg(R0 + 24576, 4096)
        tiD = c1D.bitcast(I32)
        stD = rgb(R0 + 28672, 8192)

        def g32(apv):
            return apv.rearrange("p (g j) -> p g j", g=32)

        for r in range(32):
            bi = r % 2
            if st["dq_cons"][bi]:
                vS.wait(sync, st["dq_cons"][bi])
            rdy = dma(sync, dD[bi], wbD[bi], w_f[r * 128:(r + 1) * 128, :])
            dD[bi].wait(vec, rdy)
            vec.tensor_reduce(out=amax, in_=g32(wbD[bi]), axis=AX.X,
                              op=ALU.max, apply_absolute_value=True)
            vec.drain()
            vec.tensor_scalar(out=scl, in0=amax, scalar1=1.0 / 7.0,
                              scalar2=None, op0=ALU.mult)
            vec.drain()
            vec.tensor_scalar(out=m0, in0=scl, scalar1=0.0, scalar2=None,
                              op0=ALU.is_le)
            vec.drain()
            vec.tensor_tensor(out=safe, in0=scl, in1=m0, op=ALU.add)
            vec.drain()
            vec.reciprocal(out=rsafe, in_=safe)
            vec.drain()
            st["dq_cons"][bi] = vS.inc(vec.tensor_tensor(
                out=g32(tqD), in0=g32(wbD[bi]),
                in1=rsafe.unsqueeze(2).broadcast_to([128, 32, 128]),
                op=ALU.mult))
            vec.drain()
            vec.tensor_copy(out=tiD, in_=tqD)
            vec.drain()
            vec.tensor_copy(out=taD, in_=tiD)
            vec.drain()
            vec.tensor_tensor(out=tdD, in0=tqD, in1=taD, op=ALU.subtract)
            vec.drain()
            vec.tensor_scalar(out=c2D, in0=tdD, scalar1=-0.5, scalar2=None,
                              op0=ALU.is_lt)
            vec.drain()
            vec.tensor_tensor(out=taD, in0=taD, in1=c2D, op=ALU.subtract)
            vec.drain()
            vec.tensor_scalar(out=c1D, in0=tdD, scalar1=0.5, scalar2=None,
                              op0=ALU.is_gt)
            vec.drain()
            vec.tensor_tensor(out=taD, in0=taD, in1=c1D, op=ALU.add)
            vec.drain()
            deq_rdy = vS.inc(vec.tensor_tensor(
                out=g32(tqD), in0=g32(taD),
                in1=scl.unsqueeze(2).broadcast_to([128, 32, 128]),
                op=ALU.mult))
            vS.wait(ten, deq_rdy)
            for g in range(8):
                b = g % 2
                bank_wait(ten, b)
                last = None
                for j in range(4):
                    dc = g * 4 + j
                    last = ten.transpose(pb[b][:, j * 128:(j + 1) * 128],
                                         tqD[:, dc * 128:(dc + 1) * 128],
                                         ident)
                tcnt = tS.inc(last)
                src3 = pb[b].rearrange("p (a b) -> p a b", a=4)
                tS.wait(sca, tcnt)
                if g == 0 and st["dq_st"] is not None:
                    gD.wait(sca, st["dq_st"])
                hi_ap = stD[:, g * 512:(g + 1) * 512].rearrange(
                    "p (a b) -> p a b", a=4)
                ev_h = aS.inc(sca.copy(out=hi_ap, in_=src3))
                aS.wait(vec, ev_h)
                lo_ap = stD[:, 4096 + g * 512:4096 + (g + 1) * 512].rearrange(
                    "p (a b) -> p a b", a=4)
                ev_l = vS.inc(vec.tensor_tensor(out=lo_ap, in0=src3,
                                                in1=hi_ap, op=ALU.subtract))
                bank_ev[b] = (vS, ev_l)
            dqtx = tS.n
            tS.wait(vec, dqtx)
            vS.wait(gps)
            aS.wait(gps)
            gps.dma_start(out=s_wf[r, :, :], in_=stD).then_inc(gD.h, 16)
            gD.n += 16
            st["dq_st"] = gD.n

        barrier([sync, ten, vec, sca, gps])
        sync.wait_ge(ccS.h, 1)
        otm = rg3(R0, 2, 4096)
        ln3h = rgb3(8192, 32, 1024)
        ln3l = rgb3(24576, 32, 1024)
        xbP = rg(40960, 4096)
        y1P = rg(45056, 4096)
        opl = [stage0, stage1]

        for blk in range(4):
            vS.wait(sca)
            aS.wait(vec)
            for oc in range(32):
                bi = oc % 2
                tS.wait(sync, st["op_cons"][bi])
                rdy = dma(sync, dOP[bi], opl[bi][:, 0:256],
                          s_rso[oc, :, blk * 256:(blk + 1) * 256])
                dOP[bi].wait(ten, rdy)
                b = oc % 2
                bank_wait(ten, b)
                last = None
                for ts in range(2):
                    last = ten.transpose(pb[b][:, ts * 128:(ts + 1) * 128],
                                         opl[bi][:, ts * 128:(ts + 1) * 128],
                                         ident)
                tcnt = tS.inc(last)
                st["op_cons"][bi] = tcnt
                use_vec = (oc % 2 == 0)
                eng = vec if use_vec else sca
                tS.wait(eng, tcnt)
                src3 = pb[b][:, 0:256].rearrange("p (a b) -> p a b", a=2)
                dst = otm[:, :, oc * 128:(oc + 1) * 128]
                ev = (eng.tensor_copy(out=dst, in_=src3) if use_vec
                      else eng.copy(out=dst, in_=src3))
                bank_ev[b] = ((vS if use_vec else aS),
                              (vS if use_vec else aS).inc(ev))
            otm_v, otm_a = vS.n, aS.n

            for ts in range(2):
                t0 = blk * 256 + ts * 128
                if st["p4_y1r"]:
                    vS.wait(sync, st["p4_y1r"])
                xrdy = dma(sync, dXP, xbP, x_res[t0:t0 + 128, :])
                ov = otm[:, ts, :]
                if ts == 0:
                    vS.wait(vec, otm_v)
                    aS.wait(vec, otm_a)
                    vS.wait(sca, otm_v)
                    aS.wait(sca, otm_a)
                rr = ln_stats(ov)
                aS.wait(vec, rr)
                vec.tensor_scalar(out=ov, in0=ov, scalar1=negm[:, 0:1],
                                  scalar2=rstd[:, 0:1],
                                  op0=ALU.add, op1=ALU.mult)
                vec.drain()
                dXP.wait(vec, xrdy)
                if st["p4_y1st"]:
                    vec.wait_ge(gY.h, st["p4_y1st"])
                y1r = vS.inc(vec.tensor_tensor(out=y1P, in0=ov, in1=xbP,
                                               op=ALU.add))
                st["p4_y1r"] = y1r
                vS.wait(gps, y1r)
                gps.dma_start(out=s_y1[t0:t0 + 128, :], in_=y1P
                              ).then_inc(gY.h, 16)
                gY.n += 16
                st["p4_y1st"] = gY.n
                vS.wait(sca, y1r)
                vec.drain()
                rr3 = ln_stats(y1P)
                aS.wait(vec, rr3)
                if st["p4_tx"]:
                    tS.wait(vec, st["p4_tx"])
                ln3r = vS.inc(vec.tensor_scalar(out=ov, in0=y1P,
                                                scalar1=negm[:, 0:1],
                                                scalar2=rstd[:, 0:1],
                                                op0=ALU.add, op1=ALU.mult))
                vS.wait(ten, ln3r)

                def hi3(g, _t0=t0):
                    return ln3h[:, g * 4:(g + 1) * 4, _t0:_t0 + 128]

                def lo3(g, _t0=t0):
                    return ln3l[:, g * 4:(g + 1) * 4, _t0:_t0 + 128]

                st["p4_tx"] = tx_strip_hl(ov, 32, hi3, lo3, banks=(2, 3))

        barrier([sync, ten, vec, sca, gps])
        wfl = [rgb(40960, 8192), rgb(45056, 8192)]
        for oc in range(32):
            bi = oc % 2
            if st["wfcons"][bi]:
                tS.wait(sync, st["wfcons"][bi])
            slrdy = dma(sync, dSL[bi], wfl[bi], s_wf[oc, :, :])
            for tb in range(2):
                k = (oc * 2 + tb) % 2
                b = 4 + k
                bank_wait(ten, b)
                dSL[bi].wait(ten, slrdy)
                last = None
                for dc in range(32):
                    hiW = wfl[bi][:, dc * 128:(dc + 1) * 128]
                    loW = wfl[bi][:, 4096 + dc * 128:4096 + (dc + 1) * 128]
                    rh = ln3h[:, dc, tb * 512:(tb + 1) * 512]
                    rl = ln3l[:, dc, tb * 512:(tb + 1) * 512]
                    ten.matmul(pb[b], hiW, rh, start=(dc == 0), stop=False)
                    ten.matmul(pb[b], hiW, rl, start=False, stop=False)
                    last = ten.matmul(pb[b], loW, rh, start=False,
                                      stop=(dc == 31))
                mm = tS.inc(last)
                if tb == 1:
                    st["wfcons"][bi] = mm
                tS.wait(sca, mm)
                if st["f_st"][k] is not None:
                    gQ[k].wait(sca, st["f_st"][k])
                ev = aS.inc(sca.activation(out=qkstage[k], in_=pb[b],
                                           func=ACT.Identity, scale=1.0,
                                           bias=bias_sb[:, oc:oc + 1]))
                bank_ev[b] = (aS, ev)
                aS.wait(gps, ev)
                gps.dma_start(out=s_ffn[oc, :, tb * 512:(tb + 1) * 512],
                              in_=qkstage[k]).then_inc(gQ[k].h, 16)
                gQ[k].n += 16
                st["f_st"][k] = gQ[k].n

        barrier([sync, ten, vec, sca, gps])
        ftm = rg3(R0, 4, 4096)
        y1b = rg(R1, 4096)
        outb = rg(R1 + 4096, 4096)
        for blk in range(2):
            vS.wait(sca)
            aS.wait(vec)
            for oc in range(32):
                bi = oc % 2
                tS.wait(sync, st["op_cons"][bi])
                rdy = dma(sync, dOP[bi], opl[bi],
                          s_ffn[oc, :, blk * 512:(blk + 1) * 512])
                dOP[bi].wait(ten, rdy)
                b = oc % 2
                bank_wait(ten, b)
                last = None
                for ts in range(4):
                    last = ten.transpose(pb[b][:, ts * 128:(ts + 1) * 128],
                                         opl[bi][:, ts * 128:(ts + 1) * 128],
                                         ident)
                tcnt = tS.inc(last)
                st["op_cons"][bi] = tcnt
                tS.wait(sca, tcnt)
                src3 = pb[b].rearrange("p (a b) -> p a b", a=4)
                dst = ftm[:, :, oc * 128:(oc + 1) * 128]
                ev = sca.copy(out=dst, in_=src3)
                bank_ev[b] = (aS, aS.inc(ev))
            ftm_v, ftm_a = vS.n, aS.n
            for ts in range(4):
                t0 = blk * 512 + ts * 128
                if st["p6_outr"]:
                    vS.wait(sync, st["p6_outr"])
                yrdy = dma(sync, dY1, y1b, s_y1[t0:t0 + 128, :])
                fv = ftm[:, ts, :]
                if ts == 0:
                    vS.wait(vec, ftm_v)
                    aS.wait(vec, ftm_a)
                    vS.wait(sca, ftm_v)
                    aS.wait(sca, ftm_a)
                rr = ln_stats(fv)
                aS.wait(vec, rr)
                vec.tensor_scalar(out=fv, in0=fv, scalar1=negm[:, 0:1],
                                  scalar2=rstd[:, 0:1],
                                  op0=ALU.add, op1=ALU.mult)
                vec.drain()
                dY1.wait(vec, yrdy)
                if blk + ts > 0:
                    vec.wait_ge(dOUT.h, dOUT.n)
                outr = vS.inc(vec.tensor_tensor(out=outb, in0=fv, in1=y1b,
                                                op=ALU.add))
                st["p6_outr"] = outr
                vS.wait(gps, outr)
                gps.dma_start(out=y_out[t0:t0 + 128, :], in_=outb
                              ).then_inc(dOUT.h, 16)
                dOUT.n += 16

        gps.wait_ge(dOUT.h, dOUT.n)
        sync.wait_ge(dOUT.h, dOUT.n)

    return nc


def kernel(L0_attn_x, L0_attn_w_q, L0_attn_w_k, L0_attn_w_v, L0_attn_w_o,
           L0_ffn_weight, L0_ffn_bias):
    x = np.ascontiguousarray(np.asarray(L0_attn_x, dtype=np.float32))
    wq = np.ascontiguousarray(np.asarray(L0_attn_w_q, dtype=np.float32))
    wk = np.ascontiguousarray(np.asarray(L0_attn_w_k, dtype=np.float32))
    wv = np.ascontiguousarray(np.asarray(L0_attn_w_v, dtype=np.float32))
    wo = np.ascontiguousarray(np.asarray(L0_attn_w_o, dtype=np.float32))
    wf = np.ascontiguousarray(np.asarray(L0_ffn_weight, dtype=np.float32))
    bb = np.ascontiguousarray(np.asarray(L0_ffn_bias, dtype=np.float32))
    ident = np.eye(128, dtype=np.float32)

    nc = build()
    in_maps = []
    for c in range(NCORE):
        b, h = c // 2, c % 2
        sl = slice(h * E, (h + 1) * E)
        in_maps.append({
            "x_all": x[b],
            "x_res": np.ascontiguousarray(x[b, h * TH:(h + 1) * TH]),
            "w_q": np.ascontiguousarray(wq[sl]),
            "w_k": np.ascontiguousarray(wk[sl]),
            "w_v": np.ascontiguousarray(wv[sl]),
            "w_o": np.ascontiguousarray(wo[:, sl]),
            "w_f": wf,
            "bias_in": bb,
            "ident_in": ident,
        })
    res = run_bass_kernel_spmd(nc, in_maps, core_ids=list(range(NCORE)))
    out = np.empty((B, S, D), dtype=np.float32)
    for c in range(NCORE):
        b, h = c // 2, c % 2
        out[b, h * TH:(h + 1) * TH] = np.asarray(res.results[c]["y_out"])
    return out
